# version 09 (final)
# baseline (speedup 1.0000x reference)
import numpy as np

import concourse.bass as bass
import concourse.mybir as mybir
from concourse.ap import AP
from concourse.bass_utils import run_bass_kernel_spmd

F32 = mybir.dt.float32
BF16 = mybir.dt.bfloat16
I8 = mybir.dt.int8
AF = mybir.ActivationFunctionType
ALU = mybir.AluOpType
AX = mybir.AxisListType

B, S, D, H, DH = 16, 2048, 1024, 16, 64
NC = 8
BPC = 2
T = S // 128
KC = S // 128
NF = D // 128

SA, SB, SC, SD, SE, SAL = -0.53393, 0.15818, 0.96228, 0.79666, 0.15252, 0.49786
LNBIAS = 2.0000002
MAGIC = 12582912.0


class Sched:
    def __init__(self, nc, es):
        self.nc = nc
        self.ops = []
        self.sems = {}
        self._es = es

    def sem(self, name):
        if name not in self.sems:
            h = self._es.enter_context(self.nc.semaphore(name))
            self.sems[name] = [h, 0]
        return name

    def tok(self, name):
        return (name, self.sems[name][1])

    def op(self, engine, build, waits=(), inc=None, inc_by=1):
        waits = [w for w in waits if w is not None]
        for (sn, v) in waits:
            assert v <= self.sems[sn][1], f"deadlock: wait {sn}>={v} scheduled={self.sems[sn][1]}"
        incs = []
        if inc is not None:
            self.sem(inc)
            self.sems[inc][1] += inc_by
            incs.append((inc, inc_by))
        self.ops.append((engine, build, waits, incs))
        return self.tok(inc) if inc is not None else None

    def dma(self, queue, out, in_, waits=(), inc=None, transpose=False):
        if transpose:
            queue = "sync"
        def build(eng, out=out, in_=in_, transpose=transpose):
            if transpose:
                return eng.dma_start_transpose(out, in_)
            return eng.dma_start(out=out, in_=in_)
        return self.op(queue, build, waits=waits, inc=inc, inc_by=16)

    def emit(self, name, eng):
        last_wait = {}
        for (engine, build, waits, incs) in self.ops:
            if engine != name:
                continue
            for (sn, v) in waits:
                if last_wait.get(sn, -1) >= v:
                    continue
                eng.wait_ge(self.sems[sn][0], v)
                last_wait[sn] = v
            inst = build(eng)
            for (sn, d) in incs:
                inst.then_inc(self.sems[sn][0], d)


def build_nc(stop=None, tap=None, stop_bi=0):
    nc = bass.Bass("TRN2", target_bir_lowering=False)

    x_in = nc.dram_tensor("x_in", [BPC, S, D], F32, kind="ExternalInput").ap()
    c_in = nc.dram_tensor("c_in", [BPC, D], F32, kind="ExternalInput").ap()
    wq_in = nc.dram_tensor("wq_in", [D, D], F32, kind="ExternalInput").ap()
    wk_in = nc.dram_tensor("wk_in", [D, D], F32, kind="ExternalInput").ap()
    wv_in = nc.dram_tensor("wv_in", [D, D], F32, kind="ExternalInput").ap()
    wo_in = nc.dram_tensor("wo_in", [D, D], F32, kind="ExternalInput").ap()
    bq_in = nc.dram_tensor("bq_in", [D], F32, kind="ExternalInput").ap()
    bk_in = nc.dram_tensor("bk_in", [D], F32, kind="ExternalInput").ap()
    bv_in = nc.dram_tensor("bv_in", [D], F32, kind="ExternalInput").ap()
    bo_in = nc.dram_tensor("bo_in", [D], F32, kind="ExternalInput").ap()
    wf_in = nc.dram_tensor("wf_in", [D, D], I8, kind="ExternalInput").ap()
    ws_in = nc.dram_tensor("ws_in", [D, 1], F32, kind="ExternalInput").ap()
    bf_in = nc.dram_tensor("bf_in", [D], F32, kind="ExternalInput").ap()
    wmod_in = nc.dram_tensor("wmod_in", [D, 6 * D], F32, kind="ExternalInput").ap()
    out_hbm = nc.dram_tensor("out", [BPC, S, D], F32, kind="ExternalOutput").ap()
    xp_scr = nc.dram_tensor("xp_scr", [BPC, S, D], F32, kind="Internal").ap()
    rows_scr = nc.dram_tensor("rows_scr", [16, D], F32, kind="Internal").ap()
    mod_scr = nc.dram_tensor("mod_scr", [BPC, 6 * D], F32, kind="Internal").ap()
    cc_in = nc.dram_tensor("cc_in", [1, 8], F32, kind="Internal").ap()
    cc_out = nc.dram_tensor("cc_out", [1, 8], F32, kind="Internal").ap()
    rsq_scr = nc.dram_tensor("rsq_scr", [16, S], F32, kind="Internal").ap()

    def dr(t, elem_off, apdims):
        return AP(t.tensor, elem_off, apdims)

    A_ = nc.alloc_sbuf_tensor
    w_sl = [A_(f"w_sl{i}", [128, NF * 512], BF16) for i in range(2)]
    fmaj = A_("fmaj", [128, NF * S], BF16)
    qT = A_("qT", [128, 4 * S], BF16)
    kT = A_("kT", [128, 4 * S], BF16)
    v_sb = A_("v_sb", [128, T * 512], BF16)
    attn = A_("attn", [128, T * D], BF16)
    scores = A_("scores", [128, 2 * S], BF16)
    chA = [A_(f"chA{i}", [128, S], F32) for i in range(2)]
    chB = A_("chB", [128, S], F32)
    chC = A_("chC", [128, S], F32)
    R24 = A_("R24", [128, 6 * D], F32)
    stg = [R24[:, 0:D], R24[:, D:2 * D]]
    bct = [R24[:, 2 * D:3 * D], R24[:, 3 * D:4 * D]]
    wst = [R24[:, 4 * D:5 * D], R24[:, 5 * D:6 * D]]
    chD = R24[:, 0:2048]
    chE = R24[:, 2048:4096]
    chL = R24[:, 4096:6144]
    bq_sb = A_("bq_sb", [128, NF], F32)
    bk_sb = A_("bk_sb", [128, NF], F32)
    sel16 = A_("sel16", [128, 128], F32)
    ones2 = A_("ones2", [128, 2], F32)
    mv = [A_(f"mv{i}", [128, 66], BF16) for i in range(2)]
    ncr_sb = A_("ncr_sb", [128, 4], F32)
    ncacc = A_("ncacc", [128, 2], F32)
    wchain = A_("wchain", [128, 80], F32)
    m_sb = A_("m_sb", [128, BPC * T], F32)
    r_sb = A_("r_sb", [128, BPC * T], F32)
    nmr_sb = A_("nmr_sb", [128, BPC * T], F32)
    st1 = A_("st1", [128, 12], F32)
    st2 = A_("st2", [128, 2], F32)
    ln1a = A_("ln1a", [128, 1], F32)
    ln1b = A_("ln1b", [128, 1], F32)
    mx_sb = A_("mx_sb", [128, 1], F32)
    mxr_sb = A_("mxr_sb", [1, 128], F32)
    gm_sb = A_("gm_sb", [1, 8], F32)
    srow = A_("srow", [1, D], F32)
    srow2 = A_("srow2", [1, D], F32)
    srow3 = A_("srow3", [1, D], F32)
    scT = A_("scT", [128, NF * BPC], F32)
    bvT_f = A_("bvT_f", [128, NF], F32)
    bvT_sb = A_("bvT_sb", [128, NF], BF16)
    w8 = A_("w8", [128, D], I8)
    xq_stage = scores[:, 0:2 * D]
    xqT_stage = scores[:, 2 * D:4 * D]
    CONST = {}
    for cname in ("lnb", "sal", "sqs", "sqb", "n65", "mh", "epsln", "one",
                  "s52", "fl60", "b26"):
        CONST[cname] = A_(f"c_{cname}", [128, 1], F32)
    CONST_VALS = {"lnb": LNBIAS, "sal": SAL, "sqs": SB, "sqb": SA + 2.0 * SB,
                  "n65": -65.0, "mh": -0.5, "epsln": 1e-5, "one": 1.0,
                  "s52": float(2.0 ** 52), "fl60": float(2.0 ** -60),
                  "b26": float(26.0 * np.log(2.0))}

    P_dot = nc.alloc_psum_tensor("P_dot", [128, 2048], F32)
    P_pv = [nc.alloc_psum_tensor(f"P_pv{i}", [128, 512], F32) for i in range(4)]

    from contextlib import ExitStack
    es = ExitStack()
    S_ = Sched(nc, es)
    ctx_noncontig = nc.allow_non_contiguous_dma("tiny bias/row strided loads")
    ctx_noncontig.__enter__()
    for s in ("s_dve", "s_act", "s_te", "q_st", "q_x0", "q_x1",
              "q_w0", "q_w1", "q_tr", "q_misc", "q_bc", "q_cc", "q_gp"):
        S_.sem(s)

    def dve(build, waits=()):
        return S_.op("dve", build, waits=waits, inc="s_dve")

    def act(build, waits=()):
        return S_.op("act", build, waits=waits, inc="s_act")

    def te(build, waits=()):
        return S_.op("te", build, waits=waits, inc="s_te")

    def dvec(build, waits=()):
        return S_.op("dve", build, waits=list(waits) + [S_.tok("s_dve")], inc="s_dve")

    _mchain = [None]

    def dma_row(out, in_, waits=()):
        t = S_.dma("sync", out, in_, waits=list(waits) + [_mchain[0]], inc="q_row")
        _mchain[0] = t
        return t

    for cname, cval in CONST_VALS.items():
        dve(lambda e, t=CONST[cname], v=cval: e.memset(t[:], float(v)))
    dve(lambda e: e.memset(ones2[:], 0.0))
    dve(lambda e: e.memset(ones2[0:64, 0:1], 1.0))
    dve(lambda e: e.memset(ones2[64:128, 1:2], 1.0))
    for i in range(2):
        dve(lambda e, i=i: e.memset(mv[i][:, 65:66], 1.0))
    t_consts = S_.tok("s_dve")

    def chunk_cols(dst_sb, src_dram, waits=()):
        trow = dma_row(srow[:], dr(src_dram, 0, [[1, 1], [1, D]]), waits=list(waits))
        t = None
        for cc_ in range(NF):
            t = dma_row(dst_sb[:, cc_:cc_ + 1], srow[0:1, cc_ * 128:(cc_ + 1) * 128], waits=[trow])
        return t
    tbq = chunk_cols(bq_sb, bq_in)
    tbk = chunk_cols(bk_sb, bk_in, waits=[tbq])
    tbvf = chunk_cols(bvT_f, bv_in, waits=[tbk])
    tbvT = dve(lambda e: e.tensor_copy(bvT_sb[:], bvT_f[:]), waits=[tbvf])

    t_tbl1 = act(lambda e: e.preload_act_table(AF.Silu))
    t_scT = None
    for b in range(BPC):
        tcl = dma_row(srow[:], dr(c_in, b * D, [[1, 1], [1, D]]), waits=[t_scT])
        tsl = act(lambda e: e.activation(srow2[:], srow[:], AF.Silu), waits=[tcl, t_tbl1, t_consts])
        for i in range(NF):
            t_scT = S_.dma("gp", scT[:, i * BPC + b: i * BPC + b + 1],
                           srow2[0:1, i * 128:(i + 1) * 128], waits=[tsl], inc="q_gp")
    def _preload_lnexp(e):
        from concourse.hw_specs import get_activation_tables
        for idx, funcs in enumerate(get_activation_tables(nc.m.arch).values()):
            if AF.Ln in funcs and AF.Exp in funcs:
                return e.add_instruction(mybir.InstLoadActFuncSet(
                    name=nc.get_next_instruction_name(), act_func_set_id=idx, ins=[], outs=[]))
        raise ValueError("no table with Ln+Exp")
    t_tbl2 = act(_preload_lnexp, waits=[t_scT])
    t_sz = dve(lambda e: e.memset(sel16[:], 0.0))
    t_or = dve(lambda e: e.memset(srow3[0:1, 0:64], 1.0))
    t_selb = None
    for combo in range(8):
        si, cch = combo // 4, combo % 4
        col = si * 8 + 2 * cch
        t1_ = S_.dma("gp", sel16[0:64, combo * 16 + col: combo * 16 + col + 1],
                     srow3[0:1, 0:64], waits=[t_sz, t_or], inc="q_gp")
        t_selb = S_.dma("gp", sel16[64:128, combo * 16 + col + 1: combo * 16 + col + 2],
                        srow3[0:1, 0:64], waits=[t_sz, t_or], inc="q_gp")

    wslot_free = [None, None]
    mm_done = None
    t_st2 = None
    for r in range(6):
        for i in range(NF):
            slot = i % 2
            w_ap = dr(wmod_in, i * 128 * 6 * D + r * D, [[6 * D, 128], [1, D]])
            tld = S_.dma("sync", wst[slot], w_ap, waits=[t_consts, wslot_free[slot]], inc=f"q_w{slot}")
            for s2 in range(2):
                mm_done = te(lambda e, i=i, s2=s2, slot=slot: e.matmul(
                    P_dot[0:BPC, s2 * 512:(s2 + 1) * 512], scT[:, i * BPC:(i + 1) * BPC],
                    wst[slot][:, s2 * 512:(s2 + 1) * 512],
                    start=(i == 0), stop=(i == NF - 1)), waits=[tld, t_scT])
            wslot_free[slot] = mm_done
        for s2 in range(2):
            tev = dve(lambda e, s2=s2: e.tensor_copy(chB[0:2, 0:512], P_dot[0:2, s2 * 512:(s2 + 1) * 512]), waits=[mm_done] + ([t_st2] if r + s2 else []))
            t_st1 = dma_row(dr(mod_scr, 0 * 6 * D + r * D + s2 * 512, [[1, 1], [1, 512]]),
                            chB[0:1, 0:512], waits=[tev])
            t_st2 = dma_row(dr(mod_scr, 1 * 6 * D + r * D + s2 * 512, [[1, 1], [1, 512]]),
                           chB[1:2, 0:512], waits=[tev])
        wslot_free = [t_st2, t_st2]
    t_modw = S_.tok("q_row")

    t_gp = None
    for gslice in (2, 5):
        for b in range(BPC):
            tldg = dma_row(srow[:], dr(mod_scr, b * 6 * D + gslice * D, [[1, 1], [1, D]]), waits=[t_modw, t_gp])
            dve(lambda e: e.memset(srow3[:], 0.0), waits=[tldg])
            dve(lambda e: e.tensor_tensor(srow3[:], srow[:], srow3[:], ALU.is_ge))
            dve(lambda e: e.tensor_scalar(srow2[:], srow[:], 0.25, None, ALU.max))
            dve(lambda e: e.tensor_scalar(srow[:], srow[:], -0.25, None, ALU.min))
            dve(lambda e: e.tensor_tensor(srow2[:], srow2[:], srow[:], ALU.subtract))
            dve(lambda e: e.tensor_tensor(srow2[:], srow2[:], srow3[:], ALU.mult))
            t4 = dve(lambda e: e.tensor_tensor(srow[:], srow[:], srow2[:], ALU.add))
            t_gp = dma_row(dr(mod_scr, b * 6 * D + gslice * D, [[1, 1], [1, D]]), srow[:], waits=[t4])
    t_gpost = t_gp

    bct_free = [None, None]
    stg_free = [None, None]

    def bcast(bti, row_idx, waits=()):
        ap = dr(rows_scr, row_idx * D, [[0, 128], [1, D]])
        t = S_.dma("gp", bct[bti], ap,
                   waits=list(waits) + [bct_free[bti], S_.tok("s_dve")], inc="q_bc")
        return t

    _rowchain = [None]

    def row_build(row_idx, src_off, addc=None, waits=()):
        tld = dma_row(srow3[:], dr(mod_scr, src_off, [[1, 1], [1, D]]),
                     waits=list(waits) + [_rowchain[0]])
        tt = tld
        if addc is not None:
            t1 = dve(lambda e: e.tensor_scalar(srow3[:], srow3[:], float(addc), None, ALU.add), waits=[tld])
            tt = t1
        tstore = dma_row(dr(rows_scr, row_idx * D, [[1, 1], [1, D]]), srow3[:], waits=[tt])
        _rowchain[0] = tstore
        return tstore

    def _finish():
        if tap == "mod":
            S_.dma("sync", dr(out_hbm, 0, [[6 * D, BPC], [1, 6 * D]]),
                   dr(mod_scr, 0, [[6 * D, BPC], [1, 6 * D]]),
                   waits=[S_.tok("q_row"), S_.tok("s_dve")], inc="q_st")
        if tap == "rows":
            S_.dma("sync", dr(out_hbm, 0, [[D, 16], [1, D]]),
                   dr(rows_scr, 0, [[D, 16], [1, D]]), waits=[S_.tok("q_row"), S_.tok("s_dve")], inc="q_st")
            S_.dma("sync", dr(out_hbm, 16 * D, [[D, 128], [1, D]]), bct[0],
                   waits=[S_.tok("q_bc"), S_.tok("s_dve")], inc="q_st")
            S_.dma("sync", dr(out_hbm, (16 + 128) * D, [[D, 128], [1, D]]), bct[1],
                   waits=[S_.tok("q_bc"), S_.tok("s_dve")], inc="q_st")
        if tap == "ncrall":
            g4 = [S_.tok("s_dve"), S_.tok("s_act")]
            S_.dma("sync", dr(out_hbm, 0, [[128, 128], [1, 128]]), sel16[:], waits=g4, inc="q_st")
        if tap == "wch":
            g3 = [S_.tok("s_dve"), S_.tok("s_act"), S_.tok("s_te")]
            S_.dma("sync", dr(out_hbm, 0, [[80, 128], [1, 80]]), wchain[:], waits=g3, inc="q_st")
            S_.dma("sync", dr(out_hbm, 80 * 128, [[4, 128], [1, 4]]), ncr_sb[:], waits=g3, inc="q_st")
            S_.dma("sync", dr(out_hbm, 84 * 128, [[2, 128], [1, 2]]), ncacc[:], waits=g3, inc="q_st")
            for q4_ in range(4):
                tcp2 = dve(lambda e, q4_=q4_: e.tensor_copy(chC[:, 0:512], P_pv[q4_][:]), waits=g3)
                S_.dma("sync", dr(out_hbm, (86 + q4_ * 4) * 128, [[512, 128], [1, 512]]),
                       chC[:, 0:512], waits=[tcp2], inc="q_st")
                g3 = g3 + [S_.tok("q_st")]
        if tap == "rsq":
            S_.dma("sync", dr(out_hbm, 0, [[S, 16], [1, S]]), dr(rsq_scr, 0, [[S, 16], [1, S]]),
                   waits=[S_.tok("q_row"), S_.tok("s_dve"), S_.tok("s_act")], inc="q_st")
        if tap == "xp":
            for tau2 in range(2 * T):
                S_.dma("sync", dr(out_hbm, tau2 * 128 * D, [[D, 128], [1, D]]),
                       dr(xp_scr, tau2 * 128 * D, [[D, 128], [1, D]]), waits=[S_.tok("q_st"), S_.tok("s_dve")], inc="q_st")
        if tap == "lnint":
            g2 = [S_.tok("s_dve"), S_.tok("s_act"), S_.tok("q_row"), S_.tok("q_tr"),
                  S_.tok("q_x0"), S_.tok("q_x1")]
            for nm_, buf_, off_ in (("chB", chB[:, 0:D], 0), ("chC", chC[:, 0:D], 128 * D),
                                    ("stg1", stg[1], 256 * D)):
                S_.dma("sync", dr(out_hbm, off_, [[D, 128], [1, D]]), buf_, waits=g2, inc="q_st")
            S_.dma("sync", dr(out_hbm, 384 * D, [[1, 128], [1, 1]]), ln1a[:], waits=g2, inc="q_st")
            S_.dma("sync", dr(out_hbm, 384 * D + 128, [[1, 128], [1, 1]]), ln1b[:], waits=g2, inc="q_st")
            S_.dma("sync", dr(out_hbm, 384 * D + 256, [[2, 128], [1, 2]]), st2[:], waits=g2, inc="q_st")
            S_.dma("sync", dr(out_hbm, 384 * D + 512, [[4, 128], [1, 4]]), st1[:, 0:4], waits=g2, inc="q_st")
            S_.dma("sync", dr(out_hbm, 390 * D, [[512, 128], [1, 512]]),
                   scores[:, 0:D].bitcast(F32), waits=g2, inc="q_st")
        if tap in ("h1T", "attnbuf", "qT", "kT", "v", "scores"):
            srcb = {"h1T": fmaj, "attnbuf": attn, "qT": qT, "kT": kT, "v": v_sb,
                    "scores": scores}[tap]
            ncols = srcb[:].shape[1]
            guard = [S_.tok("s_dve"), S_.tok("s_act"), S_.tok("q_row"), S_.tok("q_tr")]
            tprev = None
            for cc2 in range(0, ncols, 2048):
                wdt = min(2048, ncols - cc2)
                tcp = dve(lambda e, cc2=cc2, wdt=wdt, srcb=srcb: e.tensor_copy(
                    chC[:, 0:wdt], srcb[:, cc2:cc2 + wdt]), waits=guard + [tprev])
                tprev = S_.dma("sync", dr(out_hbm, cc2 * 128, [[wdt, 128], [1, wdt]]),
                               chC[:, 0:wdt], waits=[tcp], inc="q_st")
        fin = S_.tok("q_st")
        S_.op("sync", lambda e: e.nop(), waits=[fin])
        with nc.Block() as block:
            @block.sync
            def _(sync):
                S_.emit("sync", sync)

            @block.gpsimd
            def _(gp):
                S_.emit("gp", gp)

            @block.vector
            def _(dv):
                S_.emit("dve", dv)

            @block.scalar
            def _(ac):
                S_.emit("act", ac)

            @block.tensor
            def _(tn):
                S_.emit("te", tn)
        ctx_noncontig.__exit__(None, None, None)
        return nc

    if stop == "p0":
        return _finish()

    def load_weight_half(w_in, half, slot, waits=()):
        tok = None
        toks = []
        for i in range(NF):
            wsl = i % 2
            w_ap = dr(w_in, i * 128 * D + half * 512, [[D, 128], [1, 512]])
            prev_cast = toks[i - 2] if i >= 2 else None
            tld = S_.dma("sync", wst[wsl][:, 0:512], w_ap, waits=list(waits) + [prev_cast],
                         inc=f"q_w{wsl}")
            tok = dve(lambda e, i=i, slot=slot, wsl=wsl: e.tensor_copy(
                w_sl[slot][:, i * 512:(i + 1) * 512], wst[wsl][:, 0:512]), waits=[tld])
            toks.append(tok)
        return tok

    t_obw = None
    for bi in range(BPC):
        r0 = row_build(bi * 8 + 0, bi * 6 * D + D, addc=1.0, waits=[t_modw, t_gpost])
        r1 = row_build(bi * 8 + 1, bi * 6 * D + 0, waits=[t_modw])
        tb0 = bcast(0, bi * 8 + 0, waits=[r0])
        tb1 = bcast(1, bi * 8 + 1, waits=[r1])

        prev_tr = None
        prev_te = None
        for tau in range(T):
            sl = tau % 2
            tx = S_.dma("sync", stg[sl], dr(x_in, (bi * S + tau * 128) * D, [[D, 128], [1, D]]), waits=[stg_free[sl]], inc=f"q_x{sl}")
            t_s1 = dve(lambda e, sl=sl: e.tensor_reduce(st1[:, 0:1], stg[sl], AX.X, ALU.add),
                       waits=[tx])
            t_s1 = dve(lambda e, sl=sl: e.tensor_tensor(chC[:, 0:D], stg[sl], stg[sl], ALU.mult))
            t_s1 = dvec(lambda e: e.tensor_reduce(st1[:, 1:2], chC[:, 0:D], AX.X, ALU.add))
            t_s1 = dvec(lambda e: e.tensor_scalar(st1[:, 0:2], st1[:, 0:2], 1.0 / D, None, ALU.mult))
            t_s1 = dvec(lambda e: e.tensor_tensor(st1[:, 2:3], st1[:, 0:1], st1[:, 0:1], ALU.mult))
            t_s1 = dvec(lambda e: e.tensor_tensor(st1[:, 1:2], st1[:, 1:2], st1[:, 2:3], ALU.subtract))
            t_s2 = dvec(lambda e: e.tensor_copy(st2[:], st1[:, 0:2]), waits=[prev_te])
            t_l = act(lambda e: e.activation(ln1a[:], st2[:, 1:2], AF.Ln, bias=CONST["epsln"][:]), waits=[t_s2, t_tbl2])
            t_e = act(lambda e: e.activation(ln1b[:], ln1a[:], AF.Exp, scale=CONST["mh"][:]))
            prev_te = t_e
            t_n1 = dvec(lambda e: e.tensor_tensor(ln1a[:], st2[:, 0:1], ln1b[:], ALU.mult), waits=[t_e])
            t_n2 = dvec(lambda e: e.tensor_scalar(ln1a[:], ln1a[:], -1.0, None, ALU.mult))
            t_xn = act(lambda e, sl=sl: e.activation(chB[:, 0:D], stg[sl], AF.Identity,
                                                     bias=ln1a[:], scale=ln1b[:]), waits=[t_n2])
            stg_free[sl] = t_xn
            t_m = dve(lambda e: e.tensor_tensor(chB[:, 0:D], chB[:, 0:D], bct[0], ALU.mult), waits=[t_xn, tb0])
            t_h = dve(lambda e: e.tensor_tensor(chC[:, 0:D], chB[:, 0:D], bct[1], ALU.add), waits=[tb1])
            t_c = dve(lambda e: e.tensor_copy(scores[:, 0:D], chC[:, 0:D]), waits=[prev_tr])
            if tap == "h1tile" and tau == 0 and bi == 0:
                S_.dma("sync", dr(out_hbm, 0, [[D, 128], [1, D]]), chC[:, 0:D], waits=[t_c], inc="q_st")
                S_.dma("sync", dr(out_hbm, 128 * D, [[D, 128], [1, D]]), chB[:, 0:D], waits=[t_c], inc="q_st")
                S_.dma("sync", dr(out_hbm, 256 * D, [[D, 128], [1, D]]), stg[sl], waits=[t_c], inc="q_st")
                S_.dma("sync", dr(out_hbm, 384 * D + 0, [[1, 128], [1, 1]]), ln1a[:], waits=[t_c], inc="q_st")
                S_.dma("sync", dr(out_hbm, 384 * D + 128, [[1, 128], [1, 1]]), ln1b[:], waits=[t_c], inc="q_st")
                S_.dma("sync", dr(out_hbm, 384 * D + 256, [[1, 128], [1, D]]), bct[0], waits=[t_c], inc="q_st")
                S_.dma("sync", dr(out_hbm, 385 * D + 256, [[1, 128], [1, D]]), bct[1], waits=[t_c], inc="q_st")
            for i in range(NF):
                prev_tr = S_.dma("gp", fmaj[:, i * S + tau * 128: i * S + tau * 128 + 128],
                                 scores[:, i * 128:(i + 1) * 128], waits=[t_c], inc="q_tr",
                                 transpose=True)
        h1T_done = prev_tr
        if stop == "ln1":
            return _finish()

        for rnd in range(2):
            round_guard = S_.tok("s_dve")
            proj_toks = {}
            tq = load_weight_half(wq_in, rnd, 0, waits=[h1T_done, round_guard])
            for (name, wi, w_slot, dest, bias_sb) in (("q", wq_in, 0, qT, bq_sb),
                                                      ("k", wk_in, 1, kT, bk_sb),
                                                      ("v", wv_in, 2, v_sb, None)):
                if name == "k":
                    tq = load_weight_half(wk_in, rnd, 1, waits=[proj_toks["q"]])
                if name == "v":
                    tq = load_weight_half(wv_in, rnd, 0, waits=[proj_toks["k"]])
                last_ev = None
                if name in ("q", "k"):
                    slot = 0 if name == "q" else 1
                    for f in range(4):
                        for s4 in range(4):
                            mm = None
                            for i in range(NF):
                                mm = te(lambda e, i=i, f=f, s4=s4, slot=slot: e.matmul(
                                    P_dot[:, s4 * 512:(s4 + 1) * 512],
                                    w_sl[slot][:, i * 512 + f * 128: i * 512 + f * 128 + 128],
                                    fmaj[:, i * S + s4 * 512: i * S + s4 * 512 + 512],
                                    start=(i == 0), stop=(i == NF - 1)), waits=[tq, h1T_done, last_ev])
                            last_ev = dve(lambda e, f=f, s4=s4, dest=dest, bias_sb=bias_sb, rnd=rnd:
                                          e.tensor_scalar(
                                              dest[:, f * S + s4 * 512: f * S + s4 * 512 + 512],
                                              P_dot[:, s4 * 512:(s4 + 1) * 512],
                                              bias_sb[:, rnd * 4 + f: rnd * 4 + f + 1], None, ALU.add), waits=[mm])
                    proj_toks[name] = last_ev
                else:
                    for tau in range(T):
                        mm = None
                        for i in range(NF):
                            mm = te(lambda e, i=i, tau=tau: e.matmul(
                                P_dot[:, (tau % 4) * 512:(tau % 4) * 512 + 512],
                                fmaj[:, i * S + tau * 128: i * S + tau * 128 + 128],
                                w_sl[0][:, i * 512:(i + 1) * 512],
                                start=(i == 0), stop=(i == NF - 1)), waits=[tq, h1T_done, last_ev])
                        last_ev = dve(lambda e, tau=tau: e.tensor_copy(
                            v_sb[:, tau * 512:(tau + 1) * 512],
                            P_dot[:, (tau % 4) * 512:(tau % 4) * 512 + 512]), waits=[mm])
                    proj_toks["v"] = last_ev
            v_done = proj_toks["v"]
            if stop == "proj":
                return _finish()

            mmss = None
            for (si, src, ptok) in ((0, qT, proj_toks["q"]), (1, kT, proj_toks["k"])):
                for cch in range(4):
                    combo = si * 4 + cch
                    tsq = act(lambda e, src=src, cch=cch: e.activation(
                        chB[:, 0:S], src[:, cch * S:(cch + 1) * S], AF.Square),
                        waits=[ptok, mmss])
                    for s4 in range(4):
                        mmss = te(lambda e, s4=s4, combo=combo: e.matmul(
                            P_dot[0:16, s4 * 512:(s4 + 1) * 512],
                            sel16[:, combo * 16:(combo + 1) * 16],
                            chB[:, s4 * 512:(s4 + 1) * 512],
                            start=(combo == 0), stop=(combo == 7)), waits=[tsq, t_selb, round_guard])
            t_mx = dve(lambda e: e.tensor_scalar(chA[1][0:16, 0:S], P_dot[0:16, 0:S], 1e-19, None,
                                                 ALU.max), waits=[mmss])
            t_ln = act(lambda e: e.activation(chA[0][0:16, 0:S], chA[1][0:16, 0:S], AF.Ln), waits=[t_mx])
            t_ex = act(lambda e: e.activation(chA[1][0:16, 0:S], chA[0][0:16, 0:S], AF.Exp,
                                              scale=CONST["mh"][0:16, :]))
            t_rw = dma_row(rsq_scr, chA[1][0:16, 0:S], waits=[t_ex])
            lastapply = None
            for (si, src) in ((0, qT), (1, kT)):
                for cch in range(4):
                    j = si * 8 + 2 * cch
                    tb_hi = S_.dma("sync", R24[0:64, 2 * D:4 * D],
                                   dr(rsq_scr, j * S, [[0, 64], [1, S]]),
                                   waits=[t_rw, lastapply], inc="q_bc")
                    tb_lo = S_.dma("sync", R24[64:128, 2 * D:4 * D],
                                   dr(rsq_scr, (j + 1) * S, [[0, 64], [1, S]]), waits=[t_rw, lastapply], inc="q_bc")
                    tmul = dve(lambda e, src=src, cch=cch: e.tensor_tensor(
                        chB[:, 0:S], src[:, cch * S:(cch + 1) * S],
                        R24[:, 2 * D:4 * D], ALU.mult), waits=[tb_hi, tb_lo])
                    lastapply = dve(lambda e, src=src, cch=cch: e.tensor_copy(
                        src[:, cch * S:(cch + 1) * S], chB[:, 0:S]))
            norm_done = lastapply
            if stop == "norm":
                return _finish()

            for hh in range(8):
                head = rnd * 8 + hh
                cchunk = hh // 2
                prow = (hh % 2) * 64
                pv_tok = {}
                pend = None
                prev_ts = None
                prev_m2 = None
                prev_evac_guard = S_.tok("s_dve")

                def emit_tail(cc):
                    par = cc % 2
                    t_ln2 = act(lambda e, par=par: e.activation(
                        chL[:, 0:S], chA[par][:, 0:S], AF.Ln, bias=CONST["one"][:]), waits=[pend_ac[0]])
                    t_sc = act(lambda e, par=par: e.activation(
                        scores[:, par * S:par * S + S], chL[:, 0:S], AF.Exp,
                        scale=CONST["n65"][:], accum_out=ncacc[:, par:par + 1]), waits=[pv_tok.get(cc - 2)])
                    t_nl = act(lambda e, par=par: e.activation(
                        ncr_sb[:, 2 + par:3 + par], ncacc[:, par:par + 1], AF.Ln,
                        scale=CONST["s52"][:], bias=CONST["fl60"][:]))
                    t_ne = act(lambda e, par=par: e.activation(
                        ncr_sb[:, par:par + 1], ncr_sb[:, 2 + par:3 + par], AF.Exp,
                        scale=CONST["mh"][:], bias=CONST["b26"][:]))
                    if tap == "ncrall" and head == 0:
                        dve(lambda e, cc=cc, par=par: e.tensor_copy(
                            sel16[:, cc:cc + 1], ncacc[:, par:par + 1]), waits=[t_ne])
                        dve(lambda e, cc=cc, par=par: e.tensor_copy(
                            sel16[:, 32 + cc:33 + cc], ncr_sb[:, 2 + par:3 + par]), waits=[t_ne])
                        dve(lambda e, cc=cc, par=par: e.tensor_copy(
                            sel16[:, 64 + cc:65 + cc], ncr_sb[:, par:par + 1]), waits=[t_ne])
                    t_mv1 = dve(lambda e, cc=cc, par=par, head=head: e.tensor_scalar(
                        mv[par][:, 0:64],
                        v_sb[:, cc * 512 + (head % 8) * 64: cc * 512 + (head % 8) * 64 + 64],
                        ncr_sb[:, par:par + 1], None, ALU.mult), waits=[t_ne, v_done, pv_tok.get(cc - 2)])
                    t_mv2 = dve(lambda e, par=par: e.tensor_copy(mv[par][:, 64:65],
                                                                 ncr_sb[:, par:par + 1]))
                    last_pv = None
                    for j in range(KC):
                        last_pv = te(lambda e, cc=cc, j=j, par=par: e.matmul(
                            P_pv[j // 4][:, (j % 4) * 128:(j % 4) * 128 + 66],
                            scores[:, par * S + j * 128: par * S + j * 128 + 128],
                            mv[par][:], start=(cc == 0 and j % 4 == 0), stop=(cc == KC - 1)), waits=[t_sc, t_mv2, prev_evac_guard])
                    pv_tok[cc] = last_pv

                pend_ac = [None]
                for c in range(KC):
                    par = c % 2
                    qk = None
                    for s4 in range(4):
                        qk = te(lambda e, c=c, s4=s4, cchunk=cchunk, prow=prow: e.matmul(
                            P_dot[:, s4 * 512:(s4 + 1) * 512],
                            kT[prow:prow + 64, cchunk * S + c * 128: cchunk * S + c * 128 + 128],
                            qT[prow:prow + 64, cchunk * S + s4 * 512: cchunk * S + s4 * 512 + 512],
                            start=True, stop=True), waits=[norm_done, prev_ts, prev_evac_guard])
                    t_ts = dve(lambda e, par=par: e.tensor_scalar(
                        chA[par][:, 0:S], P_dot[:], 1.0, -2.0, ALU.min, ALU.mult), waits=[qk, pend_ac[0] if c >= 2 else None])
                    prev_ts = t_ts
                    t_ln1 = act(lambda e, par=par: e.activation(
                        chB[:, 0:S], chA[par][:, 0:S], AF.Ln, bias=CONST["lnb"][:]), waits=[t_ts])
                    t_ex1 = act(lambda e, par=par: e.activation(
                        chC[:, 0:S], chB[:, 0:S], AF.Exp, scale=CONST["sal"][:]))
                    t_sq = act(lambda e, par=par: e.activation(
                        chD[:, 0:S], chA[par][:, 0:S], AF.Square,
                        scale=CONST["sqs"][:], bias=CONST["sqb"][:]), waits=[prev_m2])
                    t_m1 = dve(lambda e, par=par: e.tensor_scalar(
                        chE[:, 0:S], chA[par][:, 0:S], SE, SD + 2 * SE, ALU.mult, ALU.add), waits=[t_ts])
                    t_m2 = dve(lambda e: e.scalar_tensor_tensor(
                        chB[:, 0:S], chD[:, 0:S], SC, chE[:, 0:S], ALU.add, ALU.mult), waits=[t_sq, t_m1])
                    prev_m2 = t_m2
                    t_ac = dve(lambda e, par=par: e.tensor_tensor(
                        chA[par][:, 0:S], chB[:, 0:S], chC[:, 0:S], ALU.mult), waits=[t_ex1, t_m2])
                    if pend is not None:
                        emit_tail(pend)
                    pend = c
                    pend_ac = [t_ac]
                emit_tail(pend)

                last_pv = pv_tok[KC - 1]
                tg = None
                for q4 in range(4):
                    for sub in range(4):
                        tg = dve(lambda e, q4=q4, sub=sub: e.tensor_copy(
                            wchain[:, q4 * 8 + sub * 2: q4 * 8 + sub * 2 + 2],
                            P_pv[q4][:, sub * 128 + 64: sub * 128 + 66]), waits=[last_pv])
                nr_ap = AP(wchain[:].tensor, wchain[:].offset + 1, [[80, 128], [2, 16]])
                tt_ap = AP(wchain[:].tensor, wchain[:].offset, [[80, 128], [2, 16]])
                t_wl = act(lambda e: e.activation(wchain[:, 32:48], nr_ap, AF.Ln,
                                                  scale=CONST["s52"][:], bias=CONST["fl60"][:]), waits=[tg])
                t_we = act(lambda e: e.activation(wchain[:, 64:80], wchain[:, 32:48], AF.Exp,
                                                  scale=CONST["mh"][:], bias=CONST["b26"][:]))
                t_z = dvec(lambda e: e.tensor_tensor(wchain[:, 32:48], tt_ap, wchain[:, 64:80],
                                                     ALU.mult), waits=[t_we])
                t_z2 = dvec(lambda e: e.tensor_scalar(wchain[:, 32:48], wchain[:, 32:48], 1e-12,
                                                      None, ALU.max))
                t_zr = dvec(lambda e: e.reciprocal(wchain[:, 48:64], wchain[:, 32:48]))
                t_w = dvec(lambda e: e.tensor_tensor(wchain[:, 48:64], wchain[:, 64:80],
                                                     wchain[:, 48:64], ALU.mult))
                for j in range(KC):
                    dve(lambda e, j=j, head=head: e.tensor_scalar(
                        attn[:, j * D + head * 64: j * D + head * 64 + 64],
                        P_pv[j // 4][:, (j % 4) * 128:(j % 4) * 128 + 64],
                        wchain[:, 48 + j:48 + j + 1], None, ALU.mult), waits=[t_w])
                if stop == f"head{head}" and bi == stop_bi:
                    return _finish()
        attn_done = S_.tok("s_dve")
        if stop == "heads":
            return _finish()

        wo_tok = None
        wtoks = []
        for i in range(NF):
            wsl = i % 2
            prev_cast = wtoks[i - 2] if i >= 2 else None
            tld = S_.dma("sync", wst[wsl][:, 0:512],
                         dr(wo_in, i * 128 * D, [[D, 128], [1, 512]]), waits=[attn_done, prev_cast], inc=f"q_w{wsl}")
            tld2 = S_.dma("sync", wst[wsl][:, 512:1024],
                          dr(wo_in, i * 128 * D + 512, [[D, 128], [1, 512]]), waits=[attn_done, prev_cast], inc=f"q_w{wsl}")
            wo_tok = dve(lambda e, i=i, wsl=wsl: e.tensor_copy(
                (w_sl[0] if i < 4 else w_sl[1])[:, (i % 4) * 1024:(i % 4) * 1024 + 1024],
                wst[wsl][:, 0:1024]), waits=[tld, tld2])
            wtoks.append(wo_tok)
        if stop == "wo":
            return _finish()
        if bi == 0:
            tev = None
            for half in range(2):
                ob_mm = None
                for i in range(NF):
                    ob_mm = te(lambda e, i=i, half=half: e.matmul(
                        P_pv[2][0:1, 0:512], bvT_sb[:, i:i + 1],
                        (w_sl[0] if i < 4 else w_sl[1])[:, (i % 4) * 1024 + half * 512:
                                                        (i % 4) * 1024 + half * 512 + 512],
                        start=(i == 0), stop=(i == NF - 1)), waits=[wo_tok, tbvT, tev])
                tev = dve(lambda e, half=half: e.tensor_copy(
                    srow[:, half * 512:(half + 1) * 512], P_pv[2][0:1, 0:512]), waits=[ob_mm, tev])
            tbo = dma_row(srow2[:], dr(bo_in, 0, [[1, 1], [1, D]]))
            tob = dve(lambda e: e.tensor_tensor(srow[:], srow[:], srow2[:], ALU.add),
                      waits=[tev, tbo])
            t_obw = dma_row(dr(rows_scr, 15 * D, [[1, 1], [1, D]]), srow[:], waits=[tob])
        if stop == "obias":
            return _finish()
        tga = dma_row(srow[:], dr(mod_scr, bi * 6 * D + 2 * D, [[1, 1], [1, D]]), waits=[t_gpost, t_obw, _rowchain[0]])
        tob2 = dma_row(srow2[:], dr(rows_scr, 15 * D, [[1, 1], [1, D]]), waits=[t_obw])
        tgaob = dve(lambda e: e.tensor_tensor(srow2[:], srow2[:], srow[:], ALU.mult),
                    waits=[tga, tob2])
        twga = dma_row(dr(rows_scr, (bi * 8 + 2) * D, [[1, 1], [1, D]]), srow[:], waits=[tgaob])
        twgaob = dma_row(dr(rows_scr, (bi * 8 + 3) * D, [[1, 1], [1, D]]), srow2[:], waits=[tgaob])
        _rowchain[0] = twgaob
        tb0 = bcast(0, bi * 8 + 2, waits=[twga, attn_done])
        tb1 = bcast(1, bi * 8 + 3, waits=[twgaob, attn_done])
        if stop == "garows":
            return _finish()

        attnT_tok = None
        for tau in range(T):
            for i in range(NF):
                attnT_tok = S_.dma("gp", fmaj[:, i * S + tau * 128: i * S + tau * 128 + 128],
                                   attn[:, tau * D + i * 128: tau * D + i * 128 + 128],
                                   waits=[attn_done], inc="q_tr", transpose=True)
        if stop == "atr":
            return _finish()
        prev_oev = [None, None]
        prev_store = [None, None]
        for tau in range(T):
            sl = tau % 2
            dest = chB if sl == 0 else chC
            tx = S_.dma("sync", stg[sl], dr(x_in, (bi * S + tau * 128) * D, [[D, 128], [1, D]]),
                        waits=[stg_free[sl], attn_done], inc=f"q_x{sl}")
            ev_toks = []
            for half in range(2):
                mm = None
                for i in range(NF):
                    mm = te(lambda e, i=i, tau=tau, half=half: e.matmul(
                        P_dot[:, half * 512:half * 512 + 512],
                        fmaj[:, i * S + tau * 128: i * S + tau * 128 + 128],
                        (w_sl[0] if i < 4 else w_sl[1])[:, (i % 4) * 1024 + half * 512:
                                                        (i % 4) * 1024 + half * 512 + 512],
                        start=(i == 0), stop=(i == NF - 1)),
                        waits=[wo_tok, attnT_tok, prev_oev[half]])
                t1 = dve(lambda e, half=half, dest=dest: e.tensor_tensor(
                    dest[:, half * 512:half * 512 + 512], P_dot[:, half * 512:half * 512 + 512],
                    bct[0][:, half * 512:half * 512 + 512], ALU.mult),
                    waits=[mm, tb0, prev_store[sl]])
                prev_oev[half] = t1
                ev_toks.append(t1)
            t2 = dve(lambda e, dest=dest: e.tensor_tensor(dest[:, 0:D], dest[:, 0:D], bct[1], ALU.add),
                     waits=ev_toks + [tb1])
            t3 = dve(lambda e, sl=sl, dest=dest: e.tensor_tensor(dest[:, 0:D], dest[:, 0:D], stg[sl], ALU.add),
                     waits=[tx])
            stg_free[sl] = t3
            prev_store[sl] = S_.dma("sync", dr(xp_scr, (bi * S + tau * 128) * D, [[D, 128], [1, D]]),
                   dest[:, 0:D], waits=[t3], inc="q_st")

    if stop == "p1":
        return _finish()
    xp_done = S_.tok("q_st")
    t_mxz = dve(lambda e: e.memset(mx_sb[:], 0.0))
    t_am = t_mxz
    for bi in range(BPC):
        r4 = row_build(bi * 8 + 4, bi * 6 * D + 4 * D, addc=1.0, waits=[t_modw])
        r5 = row_build(bi * 8 + 5, bi * 6 * D + 3 * D, waits=[t_modw])
        tb0 = bcast(0, bi * 8 + 4, waits=[r4, xp_done])
        tb1 = bcast(1, bi * 8 + 5, waits=[r5, xp_done])
        prev_te = None
        for tau in range(T):
            sl = tau % 2
            idx = bi * T + tau
            tx = S_.dma("sync", stg[sl], dr(xp_scr, (bi * S + tau * 128) * D, [[D, 128], [1, D]]),
                        waits=[stg_free[sl], xp_done], inc=f"q_x{sl}")
            t_s1 = dve(lambda e, sl=sl: e.tensor_reduce(st1[:, 0:1], stg[sl], AX.X, ALU.add),
                       waits=[tx])
            t_s1 = dve(lambda e, sl=sl: e.tensor_tensor(chC[:, 0:D], stg[sl], stg[sl], ALU.mult))
            t_s1 = dvec(lambda e: e.tensor_reduce(st1[:, 1:2], chC[:, 0:D], AX.X, ALU.add))
            t_s1 = dvec(lambda e: e.tensor_scalar(st1[:, 0:2], st1[:, 0:2], 1.0 / D, None, ALU.mult))
            t_s1 = dvec(lambda e: e.tensor_tensor(st1[:, 2:3], st1[:, 0:1], st1[:, 0:1], ALU.mult))
            t_s1 = dvec(lambda e: e.tensor_tensor(st1[:, 1:2], st1[:, 1:2], st1[:, 2:3], ALU.subtract))
            t_s2 = dvec(lambda e: e.tensor_copy(st2[:], st1[:, 0:2]), waits=[prev_te])
            t_l = act(lambda e: e.activation(ln1a[:], st2[:, 1:2], AF.Ln, bias=CONST["epsln"][:]),
                      waits=[t_s2])
            t_e = act(lambda e, idx=idx: e.activation(r_sb[:, idx:idx + 1], ln1a[:], AF.Exp,
                                                      scale=CONST["mh"][:]))
            prev_te = t_e
            t_n1 = dvec(lambda e, idx=idx: e.tensor_tensor(nmr_sb[:, idx:idx + 1], st2[:, 0:1],
                                                           r_sb[:, idx:idx + 1], ALU.mult),
                        waits=[t_e])
            t_n2 = dvec(lambda e, idx=idx: e.tensor_scalar(nmr_sb[:, idx:idx + 1],
                                                           nmr_sb[:, idx:idx + 1], -1.0, None,
                                                           ALU.mult))
            t_xn = act(lambda e, sl=sl, idx=idx: e.activation(
                chB[:, 0:D], stg[sl], AF.Identity, bias=nmr_sb[:, idx:idx + 1],
                scale=r_sb[:, idx:idx + 1]), waits=[t_n2])
            stg_free[sl] = t_xn
            t_m = dve(lambda e: e.tensor_tensor(chB[:, 0:D], chB[:, 0:D], bct[0], ALU.mult),
                      waits=[t_xn, tb0])
            t_h = dve(lambda e: e.tensor_tensor(chB[:, 0:D], chB[:, 0:D], bct[1], ALU.add),
                      waits=[tb1])
            t_r = dve(lambda e: e.tensor_reduce(chC[:, 0:1], chB[:, 0:D], AX.X, ALU.max,
                                                apply_absolute_value=True))
            t_am = dvec(lambda e: e.tensor_tensor(mx_sb[:], mx_sb[:], chC[:, 0:1], ALU.max),
                        waits=[t_am])
    t_tr = S_.dma("gp", mxr_sb[:], mx_sb[:], waits=[t_am], inc="q_tr")
    t_g1 = dve(lambda e: e.tensor_reduce(gm_sb[:, 0:1], mxr_sb[:], AX.X, ALU.max), waits=[t_tr])
    t_g2 = dvec(lambda e: e.tensor_copy(gm_sb[:, 1:8], gm_sb[:, 0:1].to_broadcast([1, 7])))
    t_cw = S_.dma("sync", cc_in, gm_sb[:], waits=[t_g2], inc="q_cc")
    t_cc = S_.op("gp", lambda e: e.collective_compute(
        "AllReduce", ALU.max, replica_groups=[[0, 1, 2, 3, 4, 5, 6, 7]],
        ins=[cc_in], outs=[cc_out]), waits=[t_cw], inc="q_gp", inc_by=1)
    t_cr = S_.dma("sync", gm_sb[:], cc_out, waits=[t_cc], inc="q_cc")
    t_s = dve(lambda e: e.tensor_scalar(gm_sb[:, 0:1], gm_sb[:, 0:1], 1.0 / 127.0, 1e-5,
                                        ALU.mult, ALU.max), waits=[t_cr])
    t_rs = dvec(lambda e: e.reciprocal(gm_sb[:, 1:2], gm_sb[:, 0:1]))

    t_ws = dma_row(srow2[:], dr(ws_in, 0, [[1, 1], [1, D]]))
    rows_done = None
    for bi in range(BPC):
        tgb = dma_row(srow[:], dr(mod_scr, bi * 6 * D + 5 * D, [[1, 1], [1, D]]), waits=[t_gpost, rows_done])
        t_a1 = dve(lambda e: e.tensor_tensor(srow3[:], srow2[:], srow[:], ALU.mult),
                   waits=[tgb, t_ws, t_rs, _rowchain[0]])
        t_a2 = dvec(lambda e: e.tensor_scalar(srow3[:], srow3[:], gm_sb[0:1, 0:1], None, ALU.mult))
        tA = dma_row(dr(rows_scr, (bi * 8 + 6) * D, [[1, 1], [1, D]]), srow3[:], waits=[t_a2])
        tbf = dma_row(srow3[:], dr(bf_in, 0, [[1, 1], [1, D]]), waits=[tA])
        t_b1 = dve(lambda e: e.tensor_tensor(srow3[:], srow3[:], srow[:], ALU.mult), waits=[tbf])
        tB = dma_row(dr(rows_scr, (bi * 8 + 7) * D, [[1, 1], [1, D]]), srow3[:], waits=[t_b1])
        _rowchain[0] = tB
        t45 = tB
        for ri in (4, 5):
            tld = dma_row(srow3[:], dr(rows_scr, (bi * 8 + ri) * D, [[1, 1], [1, D]]), waits=[t45, _rowchain[0]])
            t_c1 = dve(lambda e: e.tensor_scalar(srow3[:], srow3[:], gm_sb[0:1, 1:2], None,
                                                 ALU.mult), waits=[tld])
            t45 = dma_row(dr(rows_scr, (bi * 8 + ri) * D, [[1, 1], [1, D]]), srow3[:], waits=[t_c1])
            _rowchain[0] = t45
        rows_done = t45

    if stop == "p2":
        return _finish()
    wt_tok = None
    tcast = None
    for i in range(NF):
        tld = S_.dma("sync", w8[:], AP(wf_in.tensor, i * 128 * D, [[D, 128], [1, D]]),
                     waits=[tcast], inc="q_w0")
        tcast = dve(lambda e: e.tensor_copy(scores[:, 2 * D:3 * D], w8[:]), waits=[tld, wt_tok])
        for j in range(NF):
            wt_tok = S_.dma("gp", fmaj[:, j * D + i * 128: j * D + i * 128 + 128],
                            scores[:, 2 * D + j * 128: 2 * D + (j + 1) * 128], waits=[tcast],
                            inc="q_tr", transpose=True)

    for bi in range(BPC):
        tb0 = bcast(0, bi * 8 + 4, waits=[rows_done])
        tb1 = bcast(1, bi * 8 + 5, waits=[rows_done])
        tbA = S_.dma("gp", chA[0][:, 0:D], dr(rows_scr, (bi * 8 + 6) * D, [[0, 128], [1, D]]),
                     waits=[rows_done, S_.tok("s_dve")], inc="q_bc")
        tbB = S_.dma("gp", chA[1][:, 0:D], dr(rows_scr, (bi * 8 + 7) * D, [[0, 128], [1, D]]),
                     waits=[rows_done, S_.tok("s_dve")], inc="q_bc")
        prev_sl = [None, None]
        prev_fev = [None, None]
        prev_q4 = None
        prev_cstore = None
        for tau in range(T):
            sl = tau % 2
            slot = tau % 2
            idx = bi * T + tau
            tx = S_.dma("sync", stg[sl], dr(xp_scr, (bi * S + tau * 128) * D, [[D, 128], [1, D]]),
                        waits=[stg_free[sl]], inc=f"q_x{sl}")
            t_xn = act(lambda e, sl=sl, idx=idx: e.activation(
                chB[:, 0:D], stg[sl], AF.Identity, bias=nmr_sb[:, idx:idx + 1],
                scale=r_sb[:, idx:idx + 1]), waits=[tx, prev_q4])
            t_q1 = dve(lambda e: e.tensor_tensor(chB[:, 0:D], chB[:, 0:D], bct[0], ALU.mult),
                       waits=[t_xn, tb0])
            t_q2 = dve(lambda e: e.tensor_tensor(chB[:, 0:D], chB[:, 0:D], bct[1], ALU.add),
                       waits=[tb1])
            t_q3 = dve(lambda e: e.tensor_scalar(chB[:, 0:D], chB[:, 0:D], MAGIC, MAGIC,
                                                 ALU.add, ALU.subtract))
            t_q4 = dve(lambda e, slot=slot: e.tensor_scalar(
                xq_stage[:, slot * D:(slot + 1) * D], chB[:, 0:D], -128.0, 127.0,
                ALU.max, ALU.min), waits=[prev_sl[slot]])
            prev_q4 = t_q4
            ttr = None
            for i in range(NF):
                ttr = S_.dma("gp", xqT_stage[:, slot * D + i * 128: slot * D + i * 128 + 128],
                             xq_stage[:, slot * D + i * 128: slot * D + i * 128 + 128],
                             waits=[t_q4], inc="q_tr", transpose=True)
            ev_toks = []
            for half in range(2):
                mm = None
                for i in range(NF):
                    mm = te(lambda e, i=i, half=half, slot=slot: e.matmul(
                        P_dot[:, half * 512:half * 512 + 512],
                        xqT_stage[:, slot * D + i * 128: slot * D + i * 128 + 128],
                        fmaj[:, i * D + half * 512: i * D + half * 512 + 512],
                        start=(i == 0), stop=(i == NF - 1)),
                        waits=[ttr, wt_tok, prev_fev[half]])
                t1 = dve(lambda e, half=half: e.tensor_tensor(
                    chC[:, half * 512:half * 512 + 512], P_dot[:, half * 512:half * 512 + 512],
                    chA[0][:, half * 512:half * 512 + 512], ALU.mult), waits=[mm, tbA, prev_cstore])
                prev_fev[half] = t1
                ev_toks.append(t1)
            prev_sl[slot] = ev_toks[-1]
            t2 = dve(lambda e: e.tensor_tensor(chC[:, 0:D], chC[:, 0:D], chA[1][:, 0:D], ALU.add),
                     waits=ev_toks + [tbB])
            t3 = dve(lambda e, sl=sl: e.tensor_tensor(chC[:, 0:D], chC[:, 0:D], stg[sl], ALU.add))
            stg_free[sl] = t3
            prev_cstore = S_.dma("sync", dr(out_hbm, (bi * S + tau * 128) * D, [[D, 128], [1, D]]),
                   chC[:, 0:D], waits=[t3], inc="q_st")

    final_tok = S_.tok("q_st")
    S_.op("sync", lambda e: e.nop(), waits=[final_tok])

    return _finish()


_NC_CACHE = None


def kernel(L0_attn_x, L0_attn_wq, L0_attn_bq, L0_attn_wk, L0_attn_bk, L0_attn_wv,
           L0_attn_bv, L0_attn_wo, L0_attn_bo, L0_ffn_weight_int8, L0_ffn_weight_scale,
           L0_ffn_bias, L0_c, L0_wmod):
    global _NC_CACHE
    if _NC_CACHE is None:
        _NC_CACHE = build_nc()
    nc = _NC_CACHE
    x = np.ascontiguousarray(L0_attn_x, dtype=np.float32)
    shared = {
        "wq_in": np.ascontiguousarray(L0_attn_wq, dtype=np.float32),
        "wk_in": np.ascontiguousarray(L0_attn_wk, dtype=np.float32),
        "wv_in": np.ascontiguousarray(L0_attn_wv, dtype=np.float32),
        "wo_in": np.ascontiguousarray(L0_attn_wo, dtype=np.float32),
        "bq_in": np.ascontiguousarray(L0_attn_bq, dtype=np.float32),
        "bk_in": np.ascontiguousarray(L0_attn_bk, dtype=np.float32),
        "bv_in": np.ascontiguousarray(L0_attn_bv, dtype=np.float32),
        "bo_in": np.ascontiguousarray(L0_attn_bo, dtype=np.float32),
        "wf_in": np.ascontiguousarray(L0_ffn_weight_int8, dtype=np.int8),
        "ws_in": np.ascontiguousarray(L0_ffn_weight_scale, dtype=np.float32),
        "bf_in": np.ascontiguousarray(L0_ffn_bias, dtype=np.float32),
        "wmod_in": np.ascontiguousarray(L0_wmod, dtype=np.float32),
    }
    in_maps = []
    for c in range(NC):
        m = dict(shared)
        m["x_in"] = x[c * BPC:(c + 1) * BPC]
        m["c_in"] = np.ascontiguousarray(L0_c[c * BPC:(c + 1) * BPC], dtype=np.float32)
        in_maps.append(m)
    res = run_bass_kernel_spmd(nc, in_maps, core_ids=list(range(NC)))
    out = np.concatenate([r["out"] for r in res.results], axis=0)
    return np.ascontiguousarray(out, dtype=np.float32)
